# version 15
# speedup vs baseline: 1.1039x; 1.0270x over previous
import numpy as np
import ml_dtypes
from contextlib import ExitStack, contextmanager

import concourse.bass as bass
import concourse.mybir as mybir
import concourse.bass_utils as _bu
from concourse.bass_utils import run_bass_kernel_spmd

F32 = mybir.dt.float32
BF16 = mybir.dt.bfloat16
I32 = mybir.dt.int32
AF = mybir.ActivationFunctionType
ALU = mybir.AluOpType
AX = mybir.AxisListType

B = 8
S = 4096
D = 2048
H = 16
HD = 128
NT = S // 128
ND = D // 128
NQG = S // 512
SEL = 1024
NST = SEL // 128
FFN = 8192
NF = FFN // 128
SCALE_ATT = float(1.0 / np.sqrt(128.0))
EXP_BIAS = -20.0
LN_EPS = 1e-5
RMS_EPS = 1e-6

_orig_run_command = _bu.run_command


def _run_command_nobirsim(argv, **kw):
    argv = ["--enable-birsim=false" if a == "--enable-birsim=true" else a
            for a in argv]
    return _orig_run_command(argv, **kw)


_bu.run_command = _run_command_nobirsim


class Sem:
    def __init__(self, ctx, nc, name):
        self.h = ctx.enter_context(nc.semaphore(name))
        self.n = 0
        self.name = name


class Prog:
    ENGS = ("sync", "vector", "scalar", "tensor", "gpsimd")

    def __init__(self):
        self.plan = {e: [] for e in self.ENGS}

    def wait(self, eng, sem, val=None):
        v = sem.n if val is None else val
        if v > 0:
            self.plan[eng].append(("w", sem.h, v))

    def op(self, eng, fn, sem=None, k=1):
        if sem is None:
            self.plan[eng].append(("o", fn, None, 0))
            return None
        sem.n += k
        self.plan[eng].append(("o", fn, sem.h, k))
        return sem.n

    def run(self, nc, ctx):
        block = ctx.enter_context(nc.Block())
        for eng in self.ENGS:
            entries = self.plan[eng]
            if not entries:
                continue

            def body(e, entries=entries):
                for ent in entries:
                    if ent[0] == "w":
                        e.wait_ge(ent[1], ent[2])
                    else:
                        instr = ent[1](e)
                        if ent[2] is not None:
                            instr.then_inc(ent[2], ent[3])
            getattr(block, eng)(body)


def host_constants():
    c = {}
    c["c_ident"] = np.eye(128, dtype=np.float32)
    c["c_tri01"] = ((128 * np.arange(4)[:, None, None]
                     + np.arange(128)[None, :, None])
                    <= np.arange(512)[None, None, :]).astype(np.float32)
    inv_freq = (10000.0 ** (-(np.arange(0, 32, 2, dtype=np.float64) / 32.0)))
    t = np.arange(S, dtype=np.float64)
    ang = np.outer(t, inv_freq)
    cos = np.cos(ang).astype(np.float32).reshape(NT, 128, 16).transpose(1, 0, 2)
    sin = np.sin(ang).astype(np.float32).reshape(NT, 128, 16).transpose(1, 0, 2)
    c["c_cos"] = np.ascontiguousarray(cos)
    c["c_sin"] = np.ascontiguousarray(sin)
    cst = np.zeros((128, 16), np.float32)
    cst[:, 0] = EXP_BIAS
    cst[:, 1] = SCALE_ATT
    cst[:, 2] = 1.0 / D
    cst[:, 3] = LN_EPS
    cst[:, 4] = RMS_EPS
    cst[:, 5] = -0.5
    cst[:, 6] = -1.0
    cst[:, 7] = 0.5
    cst[:, 8] = 1025.0
    cst[:, 9] = 1.0
    c["c_cst"] = cst
    c["c_ones_r"] = np.ones((1, 512), np.float32)
    c["c_ones_rb"] = np.ones((1, 512), ml_dtypes.bfloat16)
    c["c_tri32"] = np.tril(np.ones((32, 32), np.float32), -1).T
    c["c_iota_tok"] = (np.arange(NT)[None, :] * 128
                       + np.arange(128)[:, None]).astype(np.int32)
    c["c_zero_i"] = np.zeros((4104, 1), np.int32)
    return c


class Builder:
    def __init__(self, debug=(), phases=("p1", "p2", "p3", "p4", "p5")):
        self.dbg = set(debug)
        self.phases = phases
        self.nc = bass.Bass("TRN2", target_bir_lowering=False)
        self.P = Prog()
        self.dbg_out = {}
        self.scopes = []
        self.final_sems = []

    def din(self, name, shape, dtype=F32):
        return self.nc.dram_tensor(name, shape, dtype, kind="ExternalInput").ap()

    def scr(self, name, shape, dtype):
        return self.nc.dram_tensor(name, shape, dtype, kind="Internal").ap()

    def sb(self, name, shape, dtype):
        return self.scopes[-1].enter_context(
            self.nc.sbuf_tensor(name, shape, dtype))

    def ps(self, name, shape, dtype=F32):
        return self.scopes[-1].enter_context(
            self.nc.psum_tensor(name, shape, dtype))

    def sem(self, name):
        return Sem(self.scopes[0], self.nc, name)

    @contextmanager
    def scope(self):
        es = ExitStack()
        self.scopes.append(es)
        try:
            yield
        finally:
            self.scopes.pop()
            es.close()

    def tap(self, name, shape, dtype=F32):
        if name in self.dbg:
            self.dbg_out[name] = self.nc.dram_tensor(
                "dbg_" + name, shape, dtype, kind="ExternalOutput").ap()
            return self.dbg_out[name]
        return None

    def vop(self, fn):
        return self.P.op("vector", fn, self.VP)

    def sop(self, fn):
        return self.P.op("scalar", fn, self.SP)

    def gop(self, fn):
        return self.P.op("gpsimd", fn, self.GP)

    def top(self, fn):
        return self.P.op("tensor", fn, self.TM)

    def top_(self, fn):
        return self.P.op("tensor", fn)

    def phase_barrier(self, dma_sems=()):
        P = self.P
        prog = {"vector": self.VP, "scalar": self.SP, "gpsimd": self.GP,
                "tensor": self.TM}
        for eng in Prog.ENGS:
            for name, sm in prog.items():
                if name != eng and sm.n > 0:
                    P.wait(eng, sm, sm.n)
            for sm in dma_sems:
                if sm.n > 0:
                    P.wait(eng, sm, sm.n)

    def vwait(self, val):
        self.P.wait("vector", self.VP, val)

    def swait(self, val):
        self.P.wait("scalar", self.SP, val)

    def gwait(self, val):
        self.P.wait("gpsimd", self.GP, val)

    def build(self):
        nc, P = self.nc, self.P
        self.x_in = self.din("x", [S, D])
        self.wqkv_in = self.din("wqkv", [3 * D, D])
        self.bqkv_in = self.din("bqkv", [1, 3 * D])
        self.mask_in = self.din("maskin", [SEL, SEL])
        self.freqs_in = self.din("freqs", [SEL, 128])
        self.wrouter_in = self.din("wrouter", [1, D])
        self.g1_in = self.din("g1", [1, D])
        self.wq_in = self.din("wq", [D, D])
        self.wk_in = self.din("wk", [D, D])
        self.wv_in = self.din("wv", [D, D])
        self.wo_in = self.din("wo", [D, D])
        self.g2_in = self.din("g2", [1, D])
        self.w1_in = self.din("w1", [D, FFN])
        self.w3_in = self.din("w3", [D, FFN])
        self.w2_in = self.din("w2", [FFN, D])
        self.c_ident = self.din("c_ident", [128, 128])
        self.c_tri01 = self.din("c_tri01", [4, 128, 512])
        self.c_cos = self.din("c_cos", [128, NT, 16])
        self.c_sin = self.din("c_sin", [128, NT, 16])
        self.c_cst = self.din("c_cst", [128, 16])
        self.c_ones_r = self.din("c_ones_r", [1, 512])
        self.c_ones_rb = self.din("c_ones_rb", [1, 512], BF16)
        self.c_tri32 = self.din("c_tri32", [32, 32])
        self.c_iota_tok = self.din("c_iota_tok", [128, NT], I32)
        self.c_zero_i = self.din("c_zero_i", [4104, 1], I32)
        self.c_fcos = self.din("c_fcos", [SEL, 64])
        self.c_fsin = self.din("c_fsin", [SEL, 64])

        self.OUT = self.nc.dram_tensor("out", [S, D], F32,
                                       kind="ExternalOutput").ap()

        self.xn_scr = self.scr("xn_scr", [S, D], BF16)
        self.wqkvb_scr = self.scr("wqkvb_scr", [3 * D, D], BF16)
        self.q_scr = self.scr("q_scr", [H, S, HD], BF16)
        self.k_scr = self.scr("k_scr", [H, S, HD], BF16)
        self.v_scr = self.scr("v_scr", [H, S, HD], BF16)
        self.attn_scr = self.scr("attn_scr", [S, D], F32)
        self.wq_blk = self.scr("wq_blk", [4, ND, 128, 512], BF16)
        self.wk_blk = self.scr("wk_blk", [4, ND, 128, 512], BF16)
        self.wv_blk = self.scr("wv_blk", [4, ND, 128, 512], BF16)
        self.wo_blk = self.scr("wo_blk", [4, ND, 128, 512], BF16)
        self.w1_blk = self.scr("w1_blk", [NF, ND, 128, 128], BF16)
        self.w3_blk = self.scr("w3_blk", [NF, ND, 128, 128], BF16)
        self.w2_blk = self.scr("w2_blk", [ND, NF, 128, 128], BF16)
        self.logits_scr = self.scr("logits_scr", [S, 1], F32)
        self.sel_scr = self.scr("sel_scr", [4104, 1], I32)
        self.sel32 = [self.scr(f"sel32_{t}", [4104, 1], I32)
                      for t in range(NT)]

        with ExitStack() as outer:
            self.scopes.append(outer)
            self.VP = self.sem("VP")
            self.SP = self.sem("SP")
            self.GP = self.sem("GP")
            self.TM = self.sem("TM")
            self.p0_consts()
            with self.scope():
                if "p1" in self.phases:
                    self.xn_d = self.sb("xn_d", [128, ND, S], BF16)
                    self.s_logit_col = self.sb("s_logit_col", [128, NT], F32)
                    with self.scope():
                        self.p1_layernorm()
                if "p2" in self.phases:
                    self.phase_barrier(self.p1_dma_sems)
                    self.p2b_pre()
                    with self.scope():
                        self.p2_qkv()
                    self.phase_barrier(self.p2_dma_sems)
                    with self.scope():
                        self.p2b_router()
            if "p3" in self.phases:
                self.phase_barrier(self.p2b_dma_sems)
                with self.scope():
                    self.p3_attention()
            if "p4" in self.phases:
                self.phase_barrier(self.p3_dma_sems)
                with self.scope():
                    self.p4_inner()
            if "p5" in self.phases:
                self.phase_barrier(self.p4_dma_sems)
                with self.scope():
                    self.p5_finish()
            self.final_waits()
            P.run(nc, outer)
            self.scopes.pop()
        return nc

    def p0_consts(self):
        P, sb = self.P, self.sb
        self.s_cst = sb("s_cst", [128, 16], F32)
        self.s_ident = sb("s_ident", [128, 128], BF16)
        self.s_identf = sb("s_identf_p", [128, 128], F32)
        self.s_selidx = sb("s_selidx", [128, NST], I32)
        self.s_rw = sb("s_rw", [128, NST], F32)
        self.s_onesr = sb("s_onesr", [1, 512], F32)
        self.s_onesrb = sb("s_onesrb", [1, 512], BF16)
        self.s_tri32 = sb("s_tri32", [32, 32], F32)
        self.s_iota = sb("s_iota", [128, NT], I32)
        self.s_scatidx = sb("s_scatidx", [128, 32], I32)
        self.s_selstage = sb("s_selstage", [128, NT, NST], I32)
        self.s_selsum = sb("s_selsum", [128, NST], I32)
        self.s_cos = sb("s_cos", [128, NT, 16], BF16)
        self.s_sin = sb("s_sin", [128, NT, 16], BF16)
        self.s_tri01 = [sb(f"s_tri01_{j}", [128, 512], BF16) for j in range(4)]

        CIN = self.sem("CIN")
        self.CIN = CIN
        with self.scope():
            s_cosf = self.sb("s_cosf", [128, NT, 16], F32)
            s_trif = self.sb("s_trif", [128, 512], F32)
            for dst, src in [
                (self.s_cst[:], self.c_cst), (self.s_identf[:], self.c_ident),
                (self.s_onesr[:], self.c_ones_r),
                (self.s_onesrb[:], self.c_ones_rb),
                (self.s_tri32[:], self.c_tri32),
                (self.s_iota[:], self.c_iota_tok), (s_cosf[:], self.c_cos),
                (self.sel_scr, self.c_zero_i),
            ] + [(t_, self.c_zero_i) for t_ in self.sel32]:
                P.op("sync", lambda e, dst=dst, src=src: e.dma_start(
                    out=dst, in_=src), CIN, 16)
            self.cin_main = CIN.n
            P.wait("vector", CIN, self.cin_main)
            self.vop(lambda e: e.tensor_copy(self.s_ident[:], self.s_identf[:]))
            cos_done = self.vop(
                lambda e: e.tensor_copy(self.s_cos[:], s_cosf[:]))
            P.wait("sync", self.VP, cos_done)
            P.op("sync", lambda e: e.dma_start(out=s_cosf[:], in_=self.c_sin),
                 CIN, 16)
            P.wait("vector", CIN, CIN.n)
            tri_prev = self.vop(lambda e: e.tensor_copy(self.s_sin[:], s_cosf[:]))
            for j in range(4):
                P.wait("sync", self.VP, tri_prev)
                P.op("sync", lambda e, j=j: e.dma_start(
                    out=s_trif[:], in_=self.c_tri01[j]), CIN, 16)
                P.wait("vector", CIN, CIN.n)
                tri_prev = self.vop(
                    lambda e, j=j: e.tensor_copy(self.s_tri01[j][:], s_trif[:]))
            self.consts_v = self.VP.n

    def p1_layernorm(self):
        P, sb, ps = self.P, self.sb, self.ps
        self.s_wrep = sb("s_wrep", [128, D], F32)
        with self.scope():
            s_wrow = self.sb("s_wrow", [1, D], F32)
            p_bc = self.ps("p_bc", [128, 512])
            CIN = self.CIN
            P.wait("sync", self.VP, self.consts_v)
            P.op("sync", lambda e: e.dma_start(out=s_wrow[:], in_=self.wrouter_in),
                 CIN, 16)
            P.wait("tensor", CIN, CIN.n)
            bc_prev = None
            for j in range(4):
                if bc_prev is not None:
                    P.wait("tensor", self.VP, bc_prev)
                t = self.top(lambda e, j=j, s_wrow=s_wrow: e.matmul(
                    p_bc[:], self.s_onesr[0:1, 0:128],
                    s_wrow[:, j * 512:(j + 1) * 512], start=True, stop=True))
                P.wait("vector", self.TM, t)
                bc_prev = self.vop(lambda e, j=j: e.tensor_copy(
                    self.s_wrep[:, j * 512:(j + 1) * 512], p_bc[:]))
            self.wrep_v = self.VP.n
        s_x = [sb(f"s_x{i}", [128, D], F32) for i in range(2)]
        s_xnb = [sb(f"s_xnb{i}", [128, D], BF16) for i in range(2)]
        s_junk = sb("s_junk", [128, D], BF16)
        s_sum = sb("s_sum", [128, 1], F32)
        s_sumsq = sb("s_sumsq", [128, 1], F32)
        s_mean = sb("s_mean", [128, 1], F32)
        s_var = sb("s_var", [128, 1], F32)
        s_rstd = sb("s_rstd", [128, 1], F32)
        s_rstd1 = sb("s_rstd1", [128, 1], F32)
        s_nmr = sb("s_nmr", [128, 1], F32)
        s_xw = sb("s_xw", [128, 1], F32)
        s_lg1 = sb("s_lg1", [128, 1], F32)
        s_lg2 = sb("s_lg2", [128, 1], F32)
        s_wsum = sb("s_wsum", [128, 1], F32)
        p_tp = [ps(f"p_tp{i}", [128, 1024], BF16) for i in range(2)]

        XIN = [self.sem(f"XIN{i}") for i in range(2)]
        BOUT = [self.sem(f"BOUT{i}") for i in range(2)]
        XNST = [self.sem(f"XNST{i}") for i in range(2)]

        s_wstage = [sb(f"s_wstage{i}", [128, 2048], F32) for i in range(2)]
        s_wbf = [sb(f"s_wbf{i}", [128, 2048], BF16) for i in range(2)]
        WLD = [self.sem(f"WLD{i}") for i in range(2)]
        WST = [self.sem(f"WST{i}") for i in range(2)]
        self._wq_i = 0
        self._wq_v = [0, 0]
        self._wq_pend = []

        def plan_wqkv_issue(k):
            for _ in range(k):
                if self._wq_i >= 48:
                    return
                i = self._wq_i
                self._wq_i += 1
                par = i % 2
                ldq = "sync" if par == 0 else "gpsimd"
                if i >= 2:
                    P.wait(ldq, self.VP, self._wq_v[par])
                P.op(ldq, lambda e, i=i, par=par: e.dma_start(
                    out=s_wstage[par][:],
                    in_=self.wqkv_in[i * 128:(i + 1) * 128, :]), WLD[par], 16)
                self._wq_pend.append((i, par, WLD[par].n))

        def plan_wqkv_retire(k):
            for _ in range(k):
                if not self._wq_pend:
                    return
                i, par, ln = self._wq_pend.pop(0)
                stq = "gpsimd" if par == 0 else "sync"
                P.wait("vector", WLD[par], ln)
                P.wait("vector", WST[par], WST[par].n)
                cv = self.vop(lambda e, par=par: e.tensor_copy(
                    s_wbf[par][:], s_wstage[par][:]))
                self._wq_v[par] = cv
                P.wait(stq, self.VP, cv)
                P.op(stq, lambda e, i=i, par=par: e.dma_start(
                    out=self.wqkvb_scr[i * 128:(i + 1) * 128, :],
                    in_=s_wbf[par][:]), WST[par], 16)

        ln_v = {}
        tp_tm = {}
        tp_evac_v = {}
        xnb_done = {}
        cst = self.s_cst
        for t in range(NT):
            par = t % 2
            if t >= 2:
                P.wait("sync", self.VP, ln_v[t - 2])
                P.wait("sync", BOUT[par], BOUT[par].n)
            elif t == 0:
                P.wait("sync", self.VP, self.wrep_v)
            P.op("sync", lambda e, t=t, par=par: e.dma_start(
                out=s_x[par][:], in_=self.x_in[t * 128:(t + 1) * 128, :]),
                XIN[par], 16)
            plan_wqkv_issue(2 if t % 2 == 0 else 1)

            P.wait("scalar", XIN[par], XIN[par].n)
            if t == 0:
                P.wait("scalar", self.CIN, self.cin_main)
            sumsq_s = self.sop(lambda e, par=par: e.activation(
                s_junk[:], s_x[par][:], AF.Square, accum_out=s_sumsq[:]))
            P.wait("vector", XIN[par], XIN[par].n)
            if t == 0:
                P.wait("vector", self.VP, self.consts_v)
            sum_v = self.vop(lambda e, par=par: e.reduce_sum(
                s_sum[:], s_x[par][:], axis=AX.X))
            self.vwait(sum_v)
            mean_v = self.vop(lambda e: e.tensor_scalar(
                s_mean[:], s_sum[:], cst[:, 2:3], None, ALU.mult))
            P.wait("vector", self.SP, sumsq_s)
            self.vwait(mean_v)
            mm_v = self.vop(lambda e: e.tensor_mul(s_var[:], s_mean[:], s_mean[:]))
            self.vwait(mm_v)
            var_v = self.vop(lambda e: e.scalar_tensor_tensor(
                s_var[:], s_sumsq[:], cst[:, 2:3], s_var[:],
                ALU.mult, ALU.subtract))
            P.wait("scalar", self.VP, var_v)
            ln_s_ = self.sop(lambda e: e.activation(
                s_rstd[:], s_var[:], AF.Ln, bias=cst[:, 3:4]))
            self.swait(ln_s_)
            rstd_s = self.sop(lambda e: e.activation(
                s_rstd[:], s_rstd[:], AF.Exp, scale=cst[:, 5:6]))
            P.wait("vector", self.SP, rstd_s)
            nmr_v = self.vop(lambda e: e.scalar_tensor_tensor(
                s_nmr[:], s_mean[:], cst[:, 6:7], s_rstd[:],
                ALU.mult, ALU.mult))
            self.vwait(nmr_v)
            rp_v = self.vop(lambda e: e.tensor_scalar(
                s_rstd1[:], s_rstd[:], 1.0, None, ALU.add))
            self.vwait(rp_v)
            if t >= 2:
                P.wait("vector", XNST[par], XNST[par].n)
                P.wait("vector", self.TM, tp_tm[t - 2])
            xnb_s = self.vop(lambda e, par=par: e.tensor_scalar(
                s_xnb[par][:], s_x[par][:], s_rstd[:, 0:1], s_nmr[:, 0:1],
                ALU.mult, ALU.add))
            xnb_done[t] = xnb_s
            if t == 0:
                P.wait("vector", self.VP, self.wrep_v)
                v = self.vop(lambda e: e.reduce_sum(
                    s_wsum[:], self.s_wrep[:], axis=AX.X))
                self.vwait(v)
            lm = self.vop(lambda e, par=par: e.tensor_mul(
                s_junk[:], s_x[par][:], self.s_wrep[:]))
            self.vwait(lm)
            v = self.vop(lambda e: e.reduce_sum(s_xw[:], s_junk[:], axis=AX.X))
            self.vwait(v)
            v = self.vop(lambda e: e.tensor_mul(s_lg1[:], s_xw[:], s_rstd[:]))
            self.vwait(v)
            v = self.vop(lambda e: e.tensor_mul(s_lg2[:], s_nmr[:], s_wsum[:]))
            self.vwait(v)
            v = self.vop(lambda e, t=t: e.tensor_add(
                self.s_logit_col[:, t:t + 1], s_lg1[:], s_lg2[:]))
            if t >= 2:
                P.wait("vector", self.VP, ln_v[t - 2])
            ln_v[t] = self.vop(lambda e, par=par: e.tensor_scalar(
                s_x[par][:], s_x[par][:], s_rstd1[:, 0:1], s_nmr[:, 0:1],
                ALU.mult, ALU.add))
            P.wait("sync", self.VP, ln_v[t])
            P.op("sync", lambda e, t=t, par=par: e.dma_start(
                out=self.OUT[t * 128:(t + 1) * 128, :], in_=s_x[par][:]),
                BOUT[par], 16)
            P.wait("gpsimd", self.VP, xnb_s)
            P.op("gpsimd", lambda e, t=t, par=par: e.dma_start(
                out=self.xn_scr[t * 128:(t + 1) * 128, :], in_=s_xnb[par][:]),
                XNST[par], 16)
            P.wait("tensor", self.VP, xnb_s)
            for half in range(2):
                idx = 2 * t + half
                if idx >= 2:
                    P.wait("tensor", self.VP, tp_evac_v[idx - 2])
                tpq = None
                for i in range(8):
                    db = half * 8 + i
                    fn = (lambda e, par=par, half=half, i=i, db=db: e.transpose(
                        p_tp[half][:, i * 128:(i + 1) * 128],
                        s_xnb[par][:, db * 128:(db + 1) * 128], self.s_ident[:]))
                    if i == 7:
                        tpq = self.top(fn)
                    else:
                        self.top_(fn)
                P.wait("vector", self.TM, tpq)
                tp_evac_v[idx] = self.vop(
                    lambda e, t=t, half=half: e.tensor_copy(
                        self.xn_d[:, half * 8:(half + 1) * 8,
                                  t * 128:(t + 1) * 128],
                        p_tp[half][:].rearrange("p (a b) -> p a b", a=8)))
            tp_tm[t] = self.TM.n
            plan_wqkv_retire(2 if t % 2 == 0 else 1)
        while self._wq_pend:
            plan_wqkv_retire(1)

        self.ln_done_v = self.VP.n
        self.ln_done_tm = self.TM.n
        self.p1_dma_sems = [XIN[0], XIN[1], BOUT[0], BOUT[1], XNST[0], XNST[1],
                            WLD[0], WLD[1], WST[0], WST[1]]
        self.final_sems += [BOUT[0], BOUT[1], XNST[0], XNST[1],
                            WST[0], WST[1]]
        self.XNST = XNST
        lg = self.tap("logits", [1, S], F32)
        P.wait("gpsimd", self.VP, self.ln_done_v)
        LGST = self.sem("LGST")
        P.op("gpsimd", lambda e: e.dma_start(
            out=self.logits_scr.rearrange("(t p) o -> p t o", p=128)[:, :, 0],
            in_=self.s_logit_col[:],
            allow_slow_non_contiguous=True), LGST, 16)
        if lg is not None:
            P.op("gpsimd", lambda e, lg=lg: e.dma_start(
                out=lg.rearrange("o (t p) -> o p t", p=128)[0],
                in_=self.s_logit_col[:], allow_slow_non_contiguous=True),
                LGST, 16)
        self.LGST = LGST
        self.p1_dma_sems.append(LGST)
        self.final_sems.append(LGST)

    def p0b_wqkv_cast(self):
        P, sb = self.P, self.sb
        s_wstage = [sb(f"s_wstage{i}", [128, 2048], F32) for i in range(2)]
        s_wbf = [sb(f"s_wbf{i}", [128, 2048], BF16) for i in range(2)]
        WLD = [self.sem(f"WLD{i}") for i in range(2)]
        WST = [self.sem(f"WST{i}") for i in range(2)]
        wcast_v = {}
        self.wqkv_store = {}
        for i in range(48):
            par = i % 2
            if i >= 2:
                P.wait("sync", self.VP, wcast_v[i - 2])
            P.op("sync", lambda e, i=i, par=par: e.dma_start(
                out=s_wstage[par][:], in_=self.wqkv_in[i * 128:(i + 1) * 128, :]),
                WLD[par], 16)
            P.wait("vector", WLD[par], WLD[par].n)
            if i >= 2:
                P.wait("vector", WST[par], WST[par].n)
            wcast_v[i] = self.vop(lambda e, par=par: e.tensor_copy(
                s_wbf[par][:], s_wstage[par][:]))
            P.wait("gpsimd", self.VP, wcast_v[i])
            P.op("gpsimd", lambda e, i=i, par=par: e.dma_start(
                out=self.wqkvb_scr[i * 128:(i + 1) * 128, :], in_=s_wbf[par][:]),
                WST[par], 16)
            self.wqkv_store[i] = (par, WST[par].n)
        self.WST = WST
        self.p0b_dma_sems = [WLD[0], WLD[1], WST[0], WST[1]]
        self.final_sems += [WST[0], WST[1]]

    p2_dma_sems = ()
    p2b_dma_sems = ()
    p3_dma_sems = ()
    p4_dma_sems = ()

    def p2_qkv(self):
        P, sb, ps = self.P, self.sb, self.ps
        wt_sb = [sb(f"wt_sb{i}", [128, ND, 512], BF16) for i in range(2)]
        qkvb = [sb(f"qkvb{i}", [128, 512], BF16) for i in range(2)]
        s_bstg = [sb(f"s_bstg{i}", [1, 512], F32) for i in range(2)]
        s_bch = [sb(f"s_bch{i}", [1, 512], BF16) for i in range(2)]
        BL2 = [self.sem(f"BL2{i}") for i in range(2)]
        p_q = [ps(f"p_q{i}", [128, 512]) for i in range(2)]
        WTL = [self.sem(f"WTL{i}") for i in range(2)]
        QST = [self.sem(f"QST{i}") for i in range(2)]
        scr_by_typ = [self.q_scr, self.k_scr, self.v_scr]
        evac_s = {}
        chunk_tm = {}
        bch_v = {}
        for c in range(12):
            pc = c % 2
            if c >= 2:
                P.wait("sync", self.TM, chunk_tm[c - 2])
            P.op("sync", lambda e, c=c, pc=pc: e.dma_start(
                out=s_bstg[pc][:],
                in_=self.bqkv_in[0:1, c * 512:(c + 1) * 512]), BL2[pc], 16)
            P.wait("vector", BL2[pc], BL2[pc].n)
            if c >= 2:
                P.wait("vector", self.TM, chunk_tm[c - 2])
            bch_v[c] = self.vop(lambda e, pc=pc: e.tensor_copy(
                s_bch[pc][:], s_bstg[pc][:]))
            for db in range(ND):
                P.op("sync", lambda e, c=c, db=db, pc=pc: e.dma_start_transpose(
                    wt_sb[pc][:, db, :],
                    self.wqkvb_scr[c * 512:(c + 1) * 512,
                                   db * 128:(db + 1) * 128]), WTL[pc], 16)
            wtl_n = WTL[pc].n
            P.wait("tensor", WTL[pc], wtl_n)
            typ, hblk = c // 4, c % 4
            for t in range(NT):
                pt = t % 2
                key_prev = (c, t - 2) if t >= 2 else ((c - 1, NT - 2 + t)
                                                      if c > 0 else None)
                if key_prev is not None:
                    P.wait("tensor", self.SP, evac_s[key_prev])
                if t == 0:
                    P.wait("tensor", self.VP, bch_v[c])
                self.top_(lambda e, pc=pc, pt=pt: e.matmul(
                    p_q[pt][:], self.s_onesrb[0:1, 0:128],
                    s_bch[pc][0:1, :],
                    start=True, stop=False))
                g = None
                for kc in range(ND):
                    fn = (lambda e, kc=kc, t=t, pc=pc, pt=pt: e.matmul(
                        p_q[pt][:], self.xn_d[:, kc, t * 128:(t + 1) * 128],
                        wt_sb[pc][:, kc, :], start=False, stop=(kc == ND - 1)))
                    if kc == ND - 1:
                        g = self.top(fn)
                    else:
                        self.top_(fn)
                P.wait("scalar", self.TM, g)
                P.wait("scalar", QST[pt], QST[pt].n)
                evac_s[(c, t)] = self.sop(lambda e, pt=pt: e.activation(
                    qkvb[pt][:], p_q[pt][:], AF.Copy))
                P.wait("gpsimd", self.SP, evac_s[(c, t)])
                P.op("gpsimd", lambda e, typ=typ, hblk=hblk, t=t, pt=pt:
                     e.dma_start(
                         out=scr_by_typ[typ][4 * hblk:4 * hblk + 4,
                                             t * 128:(t + 1) * 128, :]
                         .rearrange("a p b -> p a b"),
                         in_=qkvb[pt][:].rearrange("p (a b) -> p a b", a=4)),
                     QST[pt], 16)
            chunk_tm[c] = self.TM.n
            self.plan_bisect(4)
        self.plan_bisect(10**9)
        self.p2_dma_sems = [WTL[0], WTL[1], QST[0], QST[1]]
        self.final_sems += [QST[0], QST[1]]

    def p2b_pre(self):
        P, sb, ps = self.P, self.sb, self.ps
        cst = self.s_cst
        s_lrep = sb("s_lrep", [128, S], F32)
        s_cjunk = sb("s_cjunk", [128, S], mybir.dt.int8)
        s_lo = [sb(f"s_lo{i}", [128, 1], F32) for i in range(2)]
        s_hi = [sb(f"s_hi{i}", [128, 1], F32) for i in range(2)]
        s_mid = sb("s_mid", [128, 1], F32)
        s_cnt = sb("s_cnt", [128, 1], F32)
        s_cmaskf = sb("s_cmaskf", [128, 1], F32)
        s_d1 = sb("s_d1", [128, 1], F32)
        p_a = ps("p_a2", [128, 512])

        LRL = self.sem("LRL")
        P.wait("sync", self.LGST, self.LGST.n)
        P.op("sync", lambda e: e.dma_start(
            out=s_lrep[0:1, :], in_=self.logits_scr.rearrange("s o -> o s")),
            LRL, 16)
        P.wait("tensor", LRL, LRL.n)
        rep_prev = None
        for j in range(8):
            if rep_prev is not None:
                P.wait("tensor", self.VP, rep_prev)
            tmm = self.top(lambda e, j=j: e.matmul(
                p_a[:], self.s_onesr[0:1, 0:128],
                s_lrep[0:1, j * 512:(j + 1) * 512], start=True, stop=True))
            P.wait("vector", self.TM, tmm)
            rep_prev = self.vop(lambda e, j=j: e.tensor_copy(
                s_lrep[:, j * 512:(j + 1) * 512], p_a[:]))
        NITER = 34
        self._bis_it = -1

        def plan_bisect(k):
            for _ in range(k):
                if self._bis_it >= NITER:
                    return
                if self._bis_it == -1:
                    self.vwait(self.VP.n)
                    v = self.vop(lambda e: e.tensor_reduce(
                        s_lo[0][:], s_lrep[:], AX.X, ALU.min))
                    self.vwait(v)
                    v = self.vop(lambda e: e.tensor_scalar(
                        s_lo[0][:], s_lo[0][:], -1.0, None, ALU.add))
                    self.vwait(v)
                    v = self.vop(lambda e: e.reduce_max(
                        s_hi[0][:], s_lrep[:], axis=AX.X))
                    self.vwait(v)
                    self._bis_it = 0
                    continue
                it = self._bis_it
                self._bis_it += 1
                a, b2 = it % 2, (it + 1) % 2
                v = self.vop(lambda e, a=a: e.scalar_tensor_tensor(
                    s_mid[:], s_lo[a][:], cst[:, 7:8], s_hi[a][:],
                    ALU.mult, ALU.bypass))
                self.vwait(v)
                v = self.vop(lambda e, a=a: e.scalar_tensor_tensor(
                    s_mid[:], s_hi[a][:], cst[:, 7:8], s_mid[:],
                    ALU.mult, ALU.add))
                self.vwait(v)
                v = self.vop(lambda e: e.tensor_scalar(
                    s_cjunk[:], s_lrep[:], s_mid[:, 0:1], None, ALU.is_gt,
                    op1=ALU.add, accum_out=s_cnt[:]))
                self.vwait(v)
                v = self.vop(lambda e: e.tensor_scalar(
                    s_cmaskf[:], s_cnt[:], float(SEL), None, ALU.is_ge))
                self.vwait(v)
                v = self.vop(lambda e, a=a: e.tensor_sub(
                    s_d1[:], s_mid[:], s_lo[a][:]))
                self.vwait(v)
                v = self.vop(lambda e, a=a, b2=b2: e.scalar_tensor_tensor(
                    s_lo[b2][:], s_d1[:], s_cmaskf[:, 0:1], s_lo[a][:],
                    ALU.mult, ALU.add))
                self.vwait(v)
                v = self.vop(lambda e, a=a: e.tensor_sub(
                    s_d1[:], s_hi[a][:], s_mid[:]))
                self.vwait(v)
                v = self.vop(lambda e, a=a, b2=b2: e.scalar_tensor_tensor(
                    s_hi[b2][:], s_d1[:], s_cmaskf[:, 0:1], s_mid[:],
                    ALU.mult, ALU.add))
                self.vwait(v)
                if self._bis_it == NITER:
                    self.bis_done_v = self.VP.n
                    self._bis_it = NITER

        self.plan_bisect = plan_bisect
        self.bis_lo = s_lo[NITER % 2]

    def p2b_router(self):
        P, sb, ps = self.P, self.sb, self.ps
        cst = self.s_cst
        s_l32 = sb("s_l32", [32, 128], F32)
        s_m32 = sb("s_m32", [32, 128], F32)
        s_scan32 = sb("s_scan32", [32, 128], F32)
        s_zero32 = sb("s_zero32", [32, 128], F32)
        s_rs32 = sb("s_rs32", [32, 1], F32)
        s_off32 = sb("s_off32", [32, 1], F32)
        s_e32 = sb("s_e32", [32, 128], F32)
        s_t32 = sb("s_t32", [32, 128], F32)
        s_idxT = sb("s_idxT", [128, 32], F32)
        s_scatidx = self.s_scatidx
        p_b = ps("p_b", [128, 128])

        SELS = self.sem("SELS")
        SELL = self.sem("SELL")

        P.wait("tensor", self.VP, self.bis_done_v)
        tmm = self.top(lambda e: e.transpose(
            p_b[0:32, 0:128], self.s_logit_col[:], self.s_identf[:]))
        P.wait("vector", self.TM, tmm)
        v = self.vop(lambda e: e.tensor_copy(s_l32[:], p_b[0:32, 0:128]))
        self.vwait(v)
        v = self.vop(lambda e: e.tensor_scalar(
            s_m32[:], s_l32[:], self.bis_lo[0:32, 0:1], None, ALU.is_gt))
        self.vwait(v)
        v = self.vop(lambda e: e.memset(s_zero32[:], 0.0))
        self.vwait(v)
        v = self.vop(lambda e: e.tensor_tensor_scan(
            s_scan32[:], s_m32[:], s_zero32[:], 0.0, ALU.add, ALU.add))
        self.vwait(v)
        v = self.vop(lambda e: e.reduce_sum(s_rs32[:], s_m32[:], axis=AX.X))
        rs_v = self.vwait(v) or self.VP.n
        P.wait("tensor", self.VP, self.VP.n)
        tmm = self.top(lambda e: e.matmul(
            p_b[0:32, 0:1], self.s_tri32[:], s_rs32[:], start=True, stop=True))
        P.wait("vector", self.TM, tmm)
        v = self.vop(lambda e: e.tensor_copy(s_off32[:], p_b[0:32, 0:1]))
        self.vwait(v)
        v = self.vop(lambda e: e.tensor_scalar(
            s_scan32[:], s_scan32[:], s_off32[:, 0:1], None, ALU.add))
        self.vwait(v)
        v = self.vop(lambda e: e.tensor_mul(s_e32[:], s_scan32[:], s_m32[:]))
        self.vwait(v)
        v = self.vop(lambda e: e.tensor_scalar(
            s_t32[:], s_m32[:], -1025.0, 1025.0, ALU.mult, op1=ALU.add))
        self.vwait(v)
        v = self.vop(lambda e: e.scalar_tensor_tensor(
            s_e32[:], s_e32[:], -1.0, s_t32[:], ALU.add, ALU.add))
        self.vwait(v)
        P.wait("tensor", self.VP, self.VP.n)
        tmm = self.top(lambda e: e.transpose(
            p_b[0:128, 0:32], s_e32[:], self.s_identf[0:32, 0:32]))
        P.wait("vector", self.TM, tmm)
        v = self.vop(lambda e: e.tensor_copy(s_idxT[:], p_b[0:128, 0:32]))
        self.vwait(v)
        v = self.vop(lambda e: e.tensor_copy(s_scatidx[:], s_idxT[:]))
        self.vwait(v)

        P.wait("gpsimd", self.VP, self.VP.n)
        P.wait("gpsimd", self.CIN, self.cin_main)
        for t in range(NT):
            P.op("gpsimd", lambda e, t=t: e.indirect_dma_start(
                out=self.sel32[t],
                out_offset=bass.IndirectOffsetOnAxis(
                    ap=s_scatidx[:, t:t + 1], axis=0),
                in_=self.s_iota[:, t:t + 1],
                in_offset=None), SELS, 16)
        sels_total = SELS.n
        s_selstage = self.s_selstage
        s_selsum = self.s_selsum

        def plan_selfinish():
            P.wait("gpsimd", SELS, sels_total)
            for t in range(NT):
                P.op("gpsimd", lambda e, t=t: e.dma_start(
                    out=s_selstage[:, t, :],
                    in_=self.sel32[t][0:SEL, :]
                    .rearrange("(p i) o -> p (i o)", p=128)), SELL, 16)
            P.wait("vector", SELL, SELL.n)
            self.vwait(self.VP.n)
            v = self.vop(lambda e: e.tensor_copy(
                s_selsum[:], s_selstage[:, 0, :]))
            for t in range(1, NT):
                self.vwait(v)
                v = self.vop(lambda e, t=t: e.tensor_add(
                    s_selsum[:], s_selsum[:], s_selstage[:, t, :]))
            self.vwait(v)
            selsum_v = self.VP.n
            P.wait("gpsimd", self.VP, selsum_v)
            P.op("gpsimd", lambda e: e.dma_start(
                out=self.sel_scr[0:SEL, :]
                .rearrange("(p i) o -> p (i o)", p=128),
                in_=s_selsum[:]), SELL, 16)
            P.wait("gpsimd", SELL, SELL.n)
            for i in range(NST):
                P.op("gpsimd", lambda e, i=i: e.dma_start(
                    out=self.s_selidx[:, i:i + 1],
                    in_=self.sel_scr[i * 128:(i + 1) * 128, :]), SELL, 16)
            P.wait("gpsimd", SELL, SELL.n)
            sel_t = self.tap("sel", [4104, 1], F32)
            if sel_t is not None:
                s_sf = s_selstage[:, 0:4, :].rearrange("p a b -> p (a b)")
                P.wait("vector", SELL, SELL.n)
                v = self.vop(lambda e: e.tensor_copy(s_sf[:, 0:NST],
                                                     self.s_selidx[:]))
                P.wait("gpsimd", self.VP, v)
                P.op("gpsimd", lambda e, sel_t=sel_t: e.dma_start(
                    out=sel_t[0:SEL, :]
                    .rearrange("(i p) o -> p i o", p=128)[:, :, 0],
                    in_=s_sf[:, 0:NST],
                    allow_slow_non_contiguous=True), RWT, 16)
        self.plan_selfinish = plan_selfinish
        self.SELL = SELL
        RWT = self.sem("RWT")
        self.p2b_dma_sems = [SELL, RWT]
        self.final_sems += [SELS, SELL, RWT]

    def plan_rw_softmax(self, p_psum):
        P, sb = self.P, self.sb
        cst = self.s_cst
        s_wsel = sb("s_wsel", [128, NST], F32)
        s_ew = sb("s_ew", [128, NST], F32)
        s_r1 = sb("s_r1", [1, 1], F32)
        s_rinv = sb("s_rinv", [128, 1], F32)
        RWL = self.sem("RWL")
        P.op("gpsimd", lambda e: e.dma_start(
            out=s_wsel[:],
            in_=self.logits_scr[0:SEL, :]
            .rearrange("(i p) o -> p (i o)", p=128),
            allow_slow_non_contiguous=True), RWL, 16)
        P.wait("gpsimd", RWL, RWL.n)
        for i in range(NST):
            P.op("gpsimd", lambda e, i=i: e.indirect_dma_start(
                out=s_wsel[:, i:i + 1],
                out_offset=None,
                in_=self.logits_scr,
                in_offset=bass.IndirectOffsetOnAxis(
                    ap=self.s_selidx[:, i:i + 1], axis=0)), RWL, 16)
        P.wait("scalar", RWL, RWL.n)
        sexp = self.sop(lambda e: e.activation(s_ew[:], s_wsel[:], AF.Exp))
        P.wait("tensor", self.SP, sexp)
        tmm = self.top(lambda e: e.matmul(
            p_psum[0:1, 0:NST], cst[:, 9:10], s_ew[:], start=True, stop=True))
        P.wait("vector", self.TM, tmm)
        v = self.vop(lambda e: e.reduce_sum(
            s_r1[:], p_psum[0:1, 0:NST], axis=AX.X))
        self.vwait(v)
        v = self.vop(lambda e: e.reciprocal(s_r1[:], s_r1[:]))
        self.vwait(v)
        P.wait("tensor", self.VP, self.VP.n)
        tmm = self.top(lambda e: e.matmul(
            p_psum[0:128, 0:1], self.s_onesr[0:1, 0:128], s_r1[:],
            start=True, stop=True))
        P.wait("vector", self.TM, tmm)
        v = self.vop(lambda e: e.tensor_copy(s_rinv[:], p_psum[0:128, 0:1]))
        self.vwait(v)
        v = self.vop(lambda e: e.tensor_scalar(
            self.s_rw[:], s_ew[:], s_rinv[:, 0:1], None, ALU.mult))
        self.vwait(v)
        rw = self.tap("rw", [128, NST], F32)
        if rw is not None:
            P.wait("gpsimd", self.VP, self.VP.n)
            P.op("gpsimd", lambda e, rw=rw: e.dma_start(
                out=rw, in_=self.s_rw[:]), RWL, 16)
        P.wait("tensor", self.VP, self.VP.n)
        self.final_sems.append(RWL)
        return RWL

    def p3_attention(self):
        P, sb, ps = self.P, self.sb, self.ps
        cst = self.s_cst
        q_tok = [sb(f"q_tok{i}", [128, NT, 128], BF16) for i in range(2)]
        k_tok = [sb(f"k_tok{i}", [128, NT, 128], BF16) for i in range(2)]
        v_aug = [sb(f"v_aug{i}", [128, NT, 132], BF16) for i in range(2)]
        q_d = [sb(f"q_d{i}", [128, S], BF16) for i in range(2)]
        k_d = [sb(f"k_d{i}", [128, S], BF16) for i in range(2)]
        pT = [sb(f"pT{i}", [128, NT, 512], BF16) for i in range(2)]
        s_t1 = sb("s_t1", [128, NT, 16], BF16)
        s_t2 = sb("s_t2", [128, NT, 16], BF16)
        s_t3 = sb("s_t3", [128, NT, 16], BF16)
        s_recip = sb("s_recip", [128, 1], F32)
        attn_st = sb("attn_st", [128, 4, 128], F32)

        p_tph = [ps(f"p_tph{i}", [128, 1024], BF16) for i in range(2)]
        psc = [ps(f"psc{i}", [128, 1024]) for i in range(2)]
        po = [ps(f"po{i}", [128, 132]) for i in range(2)]

        QL = [self.sem(f"QL{i}") for i in range(2)]
        AOUT = [self.sem(f"AOUT{i}") for i in range(4)]
        SC = self.sem("SC")
        EXPD = self.sem("EXPD")
        TRI = self.sem("TRI")
        AV = self.sem("AV")
        NORM = self.sem("NORM")

        v = self.vop(lambda e: e.memset(v_aug[0][:, :, 128:129], 1.0))
        self.vwait(v)
        v = self.vop(lambda e: e.memset(v_aug[1][:, :, 128:129], 1.0))
        vones_v = self.VP.n

        wc_stage = [sb(f"wc_stage{i}", [128, D], F32) for i in range(2)]
        wc_bf = [sb(f"wc_bf{i}", [128, D], BF16) for i in range(2)]
        WCL = [self.sem(f"WCL{i}") for i in range(2)]
        WCS = [self.sem(f"WCS{i}") for i in range(2)]
        wc_units = []
        for win, blk in [(self.wq_in, self.wq_blk), (self.wk_in, self.wk_blk),
                         (self.wv_in, self.wv_blk), (self.wo_in, self.wo_blk)]:
            for r in range(ND):
                src = win[r * 128:(r + 1) * 128, :]
                dst = blk[0:4, r].rearrange("c p b -> p c b")
                wc_units.append((src, dst, 4))
        for win, blk in [(self.w1_in, self.w1_blk), (self.w3_in, self.w3_blk)]:
            for r in range(ND):
                for c in range(4):
                    src = win[r * 128:(r + 1) * 128, c * 2048:(c + 1) * 2048]
                    dst = blk[16 * c:16 * c + 16, r].rearrange("f p b -> p f b")
                    wc_units.append((src, dst, 16))
        for r in range(NF):
            src = self.w2_in[r * 128:(r + 1) * 128, :]
            dst = self.w2_blk[0:ND, r].rearrange("d p b -> p d b")
            wc_units.append((src, dst, ND))
        self._wc_i = 0
        self._wc_v = [0, 0]

        self._wc_pend = []

        def plan_wc_issue(k):
            for _ in range(k):
                if self._wc_i >= len(wc_units):
                    return
                i = self._wc_i
                self._wc_i += 1
                src, dst, a = wc_units[i]
                par = i % 2
                ldq = "sync" if par == 0 else "gpsimd"
                if i >= 2:
                    P.wait(ldq, self.VP, self._wc_v[par])
                P.op(ldq, lambda e, src=src, par=par: e.dma_start(
                    out=wc_stage[par][:], in_=src), WCL[par], 16)
                self._wc_pend.append((par, WCL[par].n, dst, a))

        def plan_wc_retire(k):
            for _ in range(k):
                if not self._wc_pend:
                    return
                par, ln, dst, a = self._wc_pend.pop(0)
                stq = "gpsimd" if par == 0 else "sync"
                P.wait("vector", WCL[par], ln)
                P.wait("vector", WCS[par], WCS[par].n)
                cv = self.vop(lambda e, par=par: e.tensor_copy(
                    wc_bf[par][:], wc_stage[par][:]))
                self._wc_v[par] = cv
                P.wait(stq, self.VP, cv)
                P.op(stq, lambda e, dst=dst, par=par, a=a: e.dma_start(
                    out=dst,
                    in_=wc_bf[par][:].rearrange("p (a b) -> p a b", a=a)),
                    WCS[par], 16)
        self.WCS = WCS

        last_exp_on_psc = [0, 0]
        last_norm_on_po = [0, 0]
        last_av_on_pT = [0, 0]
        sc_last = {}
        rope_v = {}
        tpe_v = {}
        exp_done = {}
        tri_done = {}
        av_done = {}
        norm_done = {}
        out_done = {}
        gqt = 0

        def plan_av_qt(h_, hp_, qg, j):
            nonlocal gqt
            qt = 4 * qg + j
            par = gqt % 2
            slot = gqt % 4
            P.wait("tensor", NORM, last_norm_on_po[par])
            P.wait("tensor", TRI, tri_done[(h_, qg, qt)])
            P.wait("tensor", EXPD, exp_done[(h_, qg, qt)])
            avn = None
            for kt in range(qt + 1):
                fn = (lambda e, qg=qg, kt=kt, j=j, par=par, qt=qt,
                      hp_=hp_: e.matmul(
                          po[par][:, 0:129],
                          pT[qg % 2][:, kt, j * 128:(j + 1) * 128],
                          v_aug[hp_][:, kt, 0:129],
                          start=(kt == 0), stop=(kt == qt)))
                if kt == qt:
                    avn = self.P.op("tensor", fn, AV)
                else:
                    self.top_(fn)
            av_done[(h_, qt)] = avn
            last_av_on_pT[qg % 2] = avn
            P.wait("vector", AV, avn)
            if gqt >= 4:
                P.wait("vector", AOUT[slot], out_done[gqt - 4])
            P.wait("vector", NORM, last_norm_on_po[1 - par]
                   if last_norm_on_po[1 - par] else 0)
            self.vwait(self.VP.n)
            v = self.vop(lambda e, par=par: e.reciprocal(
                s_recip[:], po[par][:, 128:129]))
            self.vwait(v)
            nn = self.P.op(
                "vector",
                lambda e, par=par, slot=slot: e.tensor_scalar(
                    attn_st[:, slot, :], po[par][:, 0:128],
                    s_recip[:, 0:1], None, ALU.mult), NORM)
            norm_done[(h_, qt)] = nn
            last_norm_on_po[par] = nn
            P.wait("sync", NORM, nn)
            out_done[gqt] = self.P.op(
                "sync",
                lambda e, qt=qt, h_=h_, slot=slot: e.dma_start(
                    out=self.attn_scr[qt * 128:(qt + 1) * 128,
                                      h_ * 128:(h_ + 1) * 128],
                    in_=attn_st[:, slot, :]), AOUT[slot], 16)
            gqt += 1

        pend_av = []

        def plan_rope(hp, buf):
            u1 = buf[:, :, 0:16]
            u2 = buf[:, :, 16:32]
            self.vwait(self.VP.n)
            v = self.vop(lambda e: e.tensor_mul(s_t1[:], u1, self.s_cos[:]))
            v = self.vop(lambda e: e.tensor_mul(s_t2[:], u2, self.s_sin[:]))
            v = self.vop(lambda e: e.tensor_mul(s_t3[:], u1, self.s_sin[:]))
            self.vwait(v)
            v = self.vop(lambda e: e.tensor_sub(u1, s_t1[:], s_t2[:]))
            self.vwait(v)
            v = self.vop(lambda e: e.tensor_mul(s_t1[:], u2, self.s_cos[:]))
            self.vwait(v)
            v = self.vop(lambda e: e.tensor_add(u2, s_t3[:], s_t1[:]))
            return v

        ql_done = {}

        def plan_loads(hh):
            hq = hh % 2
            if hh >= 2:
                P.wait("sync", AV, av_done[(hh - 2, NT - 1)])
                P.wait("sync", self.VP, rope_v[hh - 2])
            for src, dst in [(self.q_scr, q_tok[hq]), (self.k_scr, k_tok[hq])]:
                P.op("sync", lambda e, src=src, dst=dst, hh=hh: e.dma_start(
                    out=dst[:],
                    in_=src[hh].rearrange("(t p) d -> p t d", p=128)),
                    QL[hq], 16)
            P.op("sync", lambda e, hh=hh, hq=hq: e.dma_start(
                out=v_aug[hq][:, :, 0:128],
                in_=self.v_scr[hh].rearrange("(t p) d -> p t d", p=128)),
                QL[hq], 16)
            ql_done[hh] = QL[hq].n

        def plan_ropes(hh):
            hq = hh % 2
            P.wait("vector", QL[hq], ql_done[hh])
            if hh == 0:
                P.wait("vector", self.VP, vones_v)
            if hh >= 2:
                P.wait("vector", AV, av_done[(hh - 2, NT - 1)])
            plan_rope(hq, q_tok[hq])
            rope_v[hh] = plan_rope(hq, k_tok[hq])

        for h in range(H):
            hp = h % 2
            if h == 0:
                plan_loads(0)
                plan_ropes(0)
            while pend_av:
                plan_av_qt(*pend_av.pop(0))
            P.wait("tensor", self.VP, rope_v[h])
            for b in range(8):
                src = q_tok[hp] if b < 4 else k_tok[hp]
                dst = q_d[hp] if b < 4 else k_d[hp]
                bpar = b % 2
                gb = h * 8 + b
                if gb >= 2:
                    P.wait("tensor", self.VP, tpe_v[gb - 2])
                tpq = None
                for i in range(8):
                    t = (b % 4) * 8 + i
                    fn = (lambda e, bpar=bpar, i=i, src=src, t=t: e.transpose(
                        p_tph[bpar][:, i * 128:(i + 1) * 128],
                        src[:, t, :], self.s_ident[:]))
                    if i == 7:
                        tpq = self.top(fn)
                    else:
                        self.top_(fn)
                P.wait("vector", self.TM, tpq)
                if h >= 2 and b < 2:
                    P.wait("vector", SC, sc_last[h - 2])
                tpe_v[gb] = self.vop(
                    lambda e, dst=dst, b=b, bpar=bpar: e.tensor_copy(
                        dst[:, (b % 4) * 1024:(b % 4 + 1) * 1024],
                        p_tph[bpar][:]))

            def plan_scores_pair(qg, kth):
                par = kth % 2
                P.wait("tensor", EXPD, last_exp_on_psc[par])
                if kth == 0:
                    P.wait("tensor", self.VP, tpe_v[h * 8 + 7])
                sc = None
                for half in range(2):
                    kt = 2 * kth + half
                    fn = (lambda e, par=par, kt=kt, qg=qg, hp=hp,
                          half=half: e.matmul(
                              psc[par][:, half * 512:(half + 1) * 512],
                              k_d[hp][:, kt * 128:(kt + 1) * 128],
                              q_d[hp][:, qg * 512:(qg + 1) * 512],
                              start=True, stop=True))
                    sc = self.P.op("tensor", fn, SC)
                P.wait("scalar", SC, sc)
                P.wait("scalar", AV, last_av_on_pT[qg % 2])
                e_n = self.P.op(
                    "scalar",
                    lambda e, qg=qg, kth=kth, par=par: e.activation(
                        pT[qg % 2][:, 2 * kth:2 * kth + 2, :]
                        .rearrange("p a b -> p (a b)"), psc[par][:],
                        AF.Exp, bias=cst[:, 0:1], scale=cst[:, 1:2]),
                    EXPD)
                for half in range(2):
                    exp_done[(h, qg, 2 * kth + half)] = e_n
                last_exp_on_psc[par] = e_n

            for qg in range(NQG):
                nkt2 = 2 * (qg + 1)
                av_j = 0
                for kth in range(nkt2):
                    plan_scores_pair(qg, kth)
                    if qg > 0:
                        want = min(4, ((kth + 1) * 4) // nkt2)
                        while av_j < want:
                            plan_av_qt(h, hp, qg - 1, av_j)
                            av_j += 1
                for j in range(4):
                    kt = 4 * qg + j
                    P.wait("vector", EXPD, exp_done[(h, qg, kt)])
                    tri_done[(h, qg, kt)] = self.P.op(
                        "vector",
                        lambda e, qg=qg, kt=kt, j=j: e.tensor_mul(
                            pT[qg % 2][:, kt, :], pT[qg % 2][:, kt, :],
                            self.s_tri01[j][:]), TRI)
                if qg == 0 and h + 1 < H:
                    plan_loads(h + 1)
                plan_wc_retire(2)
                plan_wc_issue(2)
            if h + 1 < H:
                plan_ropes(h + 1)
            if h == H - 1:
                for j in range(4):
                    plan_av_qt(h, hp, NQG - 1, j)
            else:
                pend_av = [(h, hp, NQG - 1, j) for j in range(4)]
            sc_last[h] = SC.n
            if h == 1:
                self.plan_selfinish()
        while self._wc_i < len(wc_units) or self._wc_pend:
            plan_wc_issue(1)
            plan_wc_retire(1)
        self.p3_dma_sems = [QL[0], QL[1], WCL[0], WCL[1], WCS[0], WCS[1]] + AOUT
        self.final_sems += AOUT

    def p4_inner(self):
        P, sb, ps = self.P, self.sb, self.ps
        cst = self.s_cst
        s_x2 = sb("s_x2", [128, NST, D], BF16)
        s_wb = [sb(f"s_wb4_{i}", [128, 8192], BF16) for i in range(2)]
        s_yfin = sb("s_yfin4", [128, D], F32)
        s_rstd4 = sb("s_rstd4", [128, 1], F32)
        s_ssq4 = sb("s_ssq4", [128, 1], F32)
        p_g = ps("p_g", [128, 512])
        p_q4 = [ps(f"p_q4_{i}", [128, 512]) for i in range(2)]
        p_t4 = [ps(f"p_t4_{i}", [128, 1024], BF16) for i in range(2)]

        GSEL = self.sem("GSEL")
        W4L = [self.sem("W4La"), self.sem("W4Lb")]
        IQS = [self.sem(f"IQS{i}") for i in range(2)]
        RWL = self.plan_rw_softmax(p_g[:])

        PRIM = self.sem("PRIM")
        P.op("gpsimd", lambda e: e.dma_start(
            out=s_x2[:],
            in_=self.xn_scr[0:SEL, :].rearrange("(t p) d -> p t d", p=128)),
            PRIM, 16)
        P.wait("gpsimd", PRIM, PRIM.n)
        P.wait("gpsimd", self.CIN, self.cin_main)
        for i in range(NST):
            P.op("gpsimd", lambda e, i=i: e.indirect_dma_start(
                out=s_x2[:, i, :],
                out_offset=None,
                in_=self.xn_scr,
                in_offset=bass.IndirectOffsetOnAxis(
                    ap=self.s_selidx[:, i:i + 1], axis=0)), GSEL, 16)
        gsel_n = GSEL.n
        filt = self.tap("filt", [SEL, D], F32)
        if filt is not None:
            s_filtf = sb("s_filtf", [128, D], F32)
            for i in range(NST):
                P.wait("vector", GSEL, gsel_n)
                if i:
                    P.wait("vector", GSEL, GSEL.n)
                v = self.vop(lambda e, i=i: e.tensor_copy(
                    s_filtf[:], s_x2[:, i, :]))
                P.wait("gpsimd", self.VP, v)
                P.op("gpsimd", lambda e, i=i, filt=filt: e.dma_start(
                    out=filt[SEL // NST * i:SEL // NST * (i + 1), :],
                    in_=s_filtf[:]), GSEL, 16)

        def g_bcast(g_in, dst_bf):
            with self.scope():
                s_growt = self.sb("s_grow_%d" % id(dst_bf), [1, D], F32)
                s_grow = s_growt[:]
                GL = self.sem("GL4_%d" % id(dst_bf))
                P.wait("sync", self.VP, self.VP.n)
                P.op("sync", lambda e: e.dma_start(out=s_grow, in_=g_in),
                     GL, 16)
                P.wait("tensor", GL, GL.n)
                prev = None
                for j in range(4):
                    if prev is not None:
                        P.wait("tensor", self.VP, prev)
                    t = self.top(lambda e, j=j, s_grow=s_grow: e.matmul(
                        p_g[:], self.s_onesr[0:1, 0:128],
                        s_grow[0:1, j * 512:(j + 1) * 512], start=True,
                        stop=True))
                    P.wait("vector", self.TM, t)
                    prev = self.vop(lambda e, j=j, dst_bf=dst_bf: e.tensor_copy(
                        dst_bf[:, j * 512:(j + 1) * 512], p_g[:]))
                return prev

        def rms_tile(src_ap, gb, dst_ap, tm_wait=0):
            self.swait(self.SP.n)
            P.wait("scalar", self.VP, self.VP.n)
            ss = self.sop(lambda e: e.activation(
                s_wb[1][:, 0:D], src_ap, AF.Square, accum_out=s_ssq4[:]))
            self.swait(ss)
            ss = self.sop(lambda e: e.activation(
                s_rstd4[:], s_ssq4[:], AF.Ln, bias=cst[:, 4:5],
                scale=cst[:, 2:3]))
            self.swait(ss)
            ss = self.sop(lambda e: e.activation(
                s_rstd4[:], s_rstd4[:], AF.Exp, scale=cst[:, 5:6]))
            P.wait("vector", self.SP, ss)
            self.vwait(self.VP.n)
            if tm_wait:
                P.wait("vector", self.TM, tm_wait)
            return self.vop(lambda e: e.scalar_tensor_tensor(
                dst_ap, src_ap, s_rstd4[:, 0:1], gb, ALU.mult, ALU.mult))

        def stream_w_chunk(w_blk, cn, sem_pair, guard):
            par = cn % 2
            guard(par)
            P.op("sync", lambda e, w_blk=w_blk, cn=cn, par=par: e.dma_start(
                out=s_wb[par][:].rearrange("p (a b) -> p a b", a=16),
                in_=w_blk[cn].rearrange("k p b -> p k b")), sem_pair[par], 16)
            wb = s_wb[par][:].rearrange("p (a b) -> p a b", a=16)
            return wb, sem_pair[par], sem_pair[par].n

        with self.scope():
            s_cosb = self.sb("s_cosb4", [128, NST, 1, 64], BF16)
            s_sinb = self.sb("s_sinb4", [128, NST, 1, 64], BF16)
            s_g1b = self.sb("s_g1b", [128, D], BF16)
            s_htile = self.sb("s_htile", [128, D], BF16)
            s_hd = self.sb("s_hd", [128, ND, SEL], BF16)
            s_qbuf = self.sb("s_qbuf", [128, NST, D], BF16)
            s_rt1 = self.sb("s_rt1", [128, 16, 64], BF16)
            s_rt2 = self.sb("s_rt2", [128, 16, 64], BF16)
            s_rt3 = self.sb("s_rt3", [128, 16, 64], BF16)

            g1v = g_bcast(self.g1_in, s_g1b)
            s_fstg = self.sb("s_fstg", [128, NST, 128], F32)
            FQL = self.sem("FQL")
            P.wait("sync", self.VP, self.VP.n)
            P.op("sync", lambda e: e.dma_start(
                out=s_fstg[:, :, 0:64],
                in_=self.c_fcos.rearrange("(t p) j -> p t j", p=128)),
                FQL, 16)
            P.op("sync", lambda e: e.dma_start(
                out=s_fstg[:, :, 64:128],
                in_=self.c_fsin.rearrange("(t p) j -> p t j", p=128)),
                FQL, 16)
            P.wait("vector", FQL, FQL.n)
            cosv = self.vop(lambda e: e.tensor_copy(
                s_cosb[:].rearrange("p t o j -> p t (o j)"),
                s_fstg[:, :, 0:64]))
            sinv = self.vop(lambda e: e.tensor_copy(
                s_sinb[:].rearrange("p t o j -> p t (o j)"),
                s_fstg[:, :, 64:128]))

            P.wait("vector", GSEL, gsel_n)
            P.wait("scalar", GSEL, gsel_n)
            P.wait("vector", self.VP, g1v)
            htp_ev = {}
            last_tpq = 0
            for i in range(NST):
                hv = rms_tile(s_x2[:, i, :], s_g1b[:], s_htile[:],
                              tm_wait=last_tpq)
                P.wait("tensor", self.VP, hv)
                for half in range(2):
                    idx = 2 * i + half
                    if idx >= 2:
                        P.wait("tensor", self.VP, htp_ev[idx - 2])
                    tpq = None
                    for j in range(8):
                        db = half * 8 + j
                        fn = (lambda e, half=half, j=j, db=db: e.transpose(
                            p_t4[half][:, j * 128:(j + 1) * 128],
                            s_htile[:, db * 128:(db + 1) * 128],
                            self.s_ident[:]))
                        tpq = self.top(fn) if j == 7 else self.top_(fn)
                    P.wait("vector", self.TM, tpq)
                    htp_ev[idx] = self.vop(
                        lambda e, i=i, half=half: e.tensor_copy(
                            s_hd[:, half * 8:(half + 1) * 8,
                                 i * 128:(i + 1) * 128],
                            p_t4[half][:].rearrange("p (a b) -> p a b", a=8)))
                last_tpq = tpq
            hd_v = self.VP.n

            def inner_rope(buf):
                q4 = buf[:].rearrange("p t (h x) -> p t h x", h=H)
                v = None
                for t in range(NST):
                    u1 = q4[:, t, :, 0:64]
                    u2 = q4[:, t, :, 64:128]
                    cb = s_cosb[:, t, :, :].to_broadcast([128, H, 64])
                    sbn = s_sinb[:, t, :, :].to_broadcast([128, H, 64])
                    self.vwait(self.VP.n)
                    v = self.vop(lambda e, u1=u1, cb=cb: e.tensor_mul(
                        s_rt1[:], u1, cb))
                    v = self.vop(lambda e, u2=u2, sbn=sbn: e.tensor_mul(
                        s_rt2[:], u2, sbn))
                    v = self.vop(lambda e, u1=u1, sbn=sbn: e.tensor_mul(
                        s_rt3[:], u1, sbn))
                    self.vwait(v)
                    v = self.vop(lambda e, u1=u1: e.tensor_sub(
                        u1, s_rt1[:], s_rt2[:]))
                    self.vwait(v)
                    v = self.vop(lambda e, u2=u2, cb=cb: e.tensor_mul(
                        s_rt1[:], u2, cb))
                    self.vwait(v)
                    v = self.vop(lambda e, u2=u2: e.tensor_add(
                        u2, s_rt3[:], s_rt1[:]))
                return v

            scr_by = [self.q_scr, self.k_scr, self.v_scr]
            w_by = [self.wq_blk, self.wk_blk, self.wv_blk]
            proj_tm = {}
            for pi in range(3):
                if pi > 0:
                    P.wait("scalar", IQS[(pi - 1) % 2], IQS[(pi - 1) % 2].n)
                ev_s = {}
                for cn in range(4):
                    gidx = pi * 4 + cn

                    def guard(par, gidx=gidx):
                        if gidx >= 2:
                            P.wait("sync", self.TM, proj_tm[gidx - 2])
                        else:
                            P.wait("sync", self.VP, hd_v)
                    wbv, wsem, wn = stream_w_chunk(w_by[pi], cn, W4L, guard)
                    P.wait("tensor", wsem, wn)
                    for tt in range(NST):
                        ptp = tt % 2
                        key = (cn, tt - 2) if tt >= 2 else (
                            (cn - 1, NST - 2 + tt) if cn > 0 else None)
                        if key is not None:
                            P.wait("tensor", self.SP, ev_s[key])
                        elif pi > 0:
                            P.wait("tensor", self.SP, self.SP.n)
                        g = None
                        for kc in range(ND):
                            fn = (lambda e, kc=kc, tt=tt, ptp=ptp, wbv=wbv:
                                  e.matmul(
                                      p_q4[ptp][:],
                                      s_hd[:, kc, tt * 128:(tt + 1) * 128],
                                      wbv[:, kc, 0:512],
                                      start=(kc == 0), stop=(kc == ND - 1)))
                            g = self.top(fn) if kc == ND - 1 else self.top_(fn)
                        P.wait("scalar", self.TM, g)
                        ev_s[(cn, tt)] = self.sop(
                            lambda e, cn=cn, tt=tt, ptp=ptp: e.activation(
                                s_qbuf[:, tt, cn * 512:(cn + 1) * 512],
                                p_q4[ptp][:], AF.Copy))
                        if tt == NST - 1:
                            proj_tm[gidx] = self.TM.n
                P.wait("vector", self.SP, self.SP.n)
                rv = inner_rope(s_qbuf) if pi < 2 else self.VP.n
                P.wait("gpsimd", self.VP, rv)
                P.wait("gpsimd", self.SP, self.SP.n)
                for i in range(NST):
                    P.op("gpsimd", lambda e, pi=pi, i=i: e.dma_start(
                        out=scr_by[pi][:, i * 128:(i + 1) * 128, :]
                        .rearrange("a p b -> p a b"),
                        in_=s_qbuf[:, i, :].rearrange("p (a b) -> p a b", a=H)),
                        IQS[pi % 2], 16)
            self.iqs_done = [IQS[0].n, IQS[1].n]

        self.phase_barrier([W4L[0], W4L[1], FQL, IQS[0], IQS[1], GSEL, PRIM])
        with self.scope():
            s_otok = self.sb("s_otok", [128, NST, D], BF16)
            with self.scope():
                s_mask = self.sb("s_mask4", [128, NST, SEL], F32)
                s_asm_a = self.sb("s_asm_a", [128, D], F32)
                s_asm_b = self.sb("s_asm_b", [128, D], F32)
                AIN = self.sem("AIN")
                ATT = self.sem("ATT")
                ASM = self.sem("ASM")
                asm_state = {"v": 0}

                def plan_asm_tile(t):
                    P.wait("sync", ASM, ASM.n)
                    P.wait("sync", self.VP, asm_state["v"])
                    P.op("sync", lambda e, t=t: e.dma_start(
                        out=s_asm_a[:],
                        in_=self.OUT[t * 128:(t + 1) * 128, :]), AIN, 16)
                    P.op("gpsimd", lambda e, t=t: e.dma_start(
                        out=s_asm_b[:],
                        in_=self.attn_scr[t * 128:(t + 1) * 128, :]), ATT, 16)
                    P.wait("vector", AIN, AIN.n)
                    P.wait("vector", ATT, ATT.n)
                    self.vwait(self.VP.n)
                    v = self.vop(lambda e: e.tensor_add(
                        s_asm_a[:], s_asm_a[:], s_asm_b[:]))
                    asm_state["v"] = v
                    P.wait("sync", self.VP, v)
                    P.op("sync", lambda e, t=t: e.dma_start(
                        out=self.OUT[t * 128:(t + 1) * 128, :],
                        in_=s_asm_a[:]), ASM, 16)
                    P.wait("gpsimd", ASM, ASM.n)
                s_qh = [self.sb(f"s_qh{i}", [128, NST, 128], BF16)
                        for i in range(2)]
                s_kh = [self.sb(f"s_kh{i}", [128, NST, 128], BF16)
                        for i in range(2)]
                s_vh = [self.sb(f"s_vh{i}", [128, NST, 128], BF16)
                        for i in range(2)]
                s_qdi = [self.sb(f"s_qdi{i}", [128, SEL], BF16)
                         for i in range(2)]
                s_kdi = [self.sb(f"s_kdi{i}", [128, SEL], BF16)
                         for i in range(2)]
                s_sc4 = [self.sb(f"s_sc4_{i}", [128, SEL], F32)
                         for i in range(2)]
                s_pin = [self.sb(f"s_pin{i}", [128, SEL], BF16)
                         for i in range(2)]
                s_pT4 = [self.sb(f"s_pT4_{i}", [128, NST, 128], BF16)
                         for i in range(2)]
                s_scp = [self.sb(f"s_scp{i}", [128, SEL], F32)
                         for i in range(2)]
                s_mx = [self.sb(f"s_mx{i}", [128, 1], F32) for i in range(2)]
                s_rs4 = [self.sb(f"s_rs4_{i}", [128, 1], F32)
                         for i in range(2)]
                s_rc4 = [self.sb(f"s_rc4_{i}", [128, 1], F32)
                         for i in range(2)]
                p_s4 = ps("p_s4", [128, 1024])
                p_o4 = ps("p_o4", [128, 128])

                MSKL = self.sem("MSKL")
                ICP = self.sem("ICP")
                IQL = [self.sem(f"IQL{i}") for i in range(2)]
                ISC = self.sem("ISC")
                IEX = self.sem("IEX")
                IAV = self.sem("IAV")
                INRM = self.sem("INRM")

                P.op("sync", lambda e: e.dma_start(
                    out=s_mask[:],
                    in_=self.mask_in.rearrange("(t p) k -> p t k", p=128)),
                    MSKL, 16)
                iq_done = {}
                itp_ev = {}
                isc_last = {}
                last_cpy = {0: 0, 1: 0}
                last_madd = {0: 0, 1: 0}
                last_exp = {0: 0, 1: 0}
                last_tp = {0: 0, 1: 0}
                last_evac = {0: 0, 1: 0}
                last_attnv = {0: 0, 1: 0}
                last_norm = {0: 0, 1: 0}
                last_rcp = {0: 0, 1: 0}
                last_rmx = {0: 0, 1: 0}

                def psS(par, half):
                    return (p_s4[:, half * 512:(half + 1) * 512]
                            if par == 0 else p_q4[half][:])

                iql_done = {}

                def plan_iloads(hh):
                    hq = hh % 2
                    if hh >= 2:
                        P.wait("sync", IAV, iq_done[hh - 2])
                    for src, dst in [(self.q_scr, s_qh[hq]),
                                     (self.k_scr, s_kh[hq]),
                                     (self.v_scr, s_vh[hq])]:
                        P.op("sync", lambda e, src=src, dst=dst, hh=hh:
                             e.dma_start(
                                 out=dst[:],
                                 in_=src[hh, 0:SEL, :]
                                 .rearrange("(t p) d -> p t d", p=128)),
                             IQL[hq], 16)
                    iql_done[hh] = IQL[hq].n

                for h in range(H):
                    hp = h % 2
                    if h == 0:
                        plan_iloads(0)
                    P.wait("tensor", IQL[hp], iql_done[h])
                    for b in range(2):
                        src = s_qh[hp] if b == 0 else s_kh[hp]
                        dst = s_qdi[hp] if b == 0 else s_kdi[hp]
                        gb = 2 * h + b
                        if gb >= 2:
                            P.wait("tensor", self.VP, itp_ev[gb - 2])
                        P.wait("tensor", self.VP, last_evac[b])
                        if h >= 2 and b == 0:
                            P.wait("tensor", ISC, isc_last[h - 2])
                        tpq = None
                        for j in range(8):
                            fn = (lambda e, b=b, j=j, src=src: e.transpose(
                                p_t4[b][:, j * 128:(j + 1) * 128],
                                src[:, j, :], self.s_ident[:]))
                            tpq = self.top(fn) if j == 7 else self.top_(fn)
                        P.wait("vector", self.TM, tpq)
                        itp_ev[gb] = self.vop(
                            lambda e, dst=dst, b=b: e.tensor_copy(
                                dst[:], p_t4[b][:]))
                    last_evac[0] = itp_ev[2 * h + 1]
                    last_evac[1] = itp_ev[2 * h + 1]
                    qstate = {}

                    def plan_qtA(qt):
                        par = qt % 2
                        if qt == 0:
                            P.wait("tensor", self.VP, itp_ev[2 * h + 1])
                        scn = None
                        for half in range(2):
                            P.wait("tensor", ICP, last_cpy[par])
                            fn = (lambda e, qt=qt, half=half, hp=hp, par=par:
                                  e.matmul(
                                      psS(par, half),
                                      s_qdi[hp][:, qt * 128:(qt + 1) * 128],
                                      s_kdi[hp][:, half * 512:(half + 1) * 512],
                                      start=True, stop=True))
                            scn = self.P.op("tensor", fn, ISC)
                        P.wait("scalar", ISC, scn)
                        P.wait("scalar", self.VP, last_madd[par])
                        cpy = None
                        if par == 0:
                            cpy = self.P.op("scalar", lambda e, par=par:
                                            e.activation(
                                                s_scp[par][:], p_s4[:],
                                                AF.Copy, scale=SCALE_ATT), ICP)
                        else:
                            for half in range(2):
                                cpy = self.P.op(
                                    "scalar",
                                    lambda e, par=par, half=half: e.activation(
                                        s_scp[par][:,
                                                   half * 512:(half + 1) * 512],
                                        p_q4[half][:], AF.Copy,
                                        scale=SCALE_ATT), ICP)
                        last_cpy[par] = cpy
                        P.wait("vector", ICP, cpy)
                        P.wait("vector", IEX, last_exp[par])
                        if h == 0 and qt == 0:
                            P.wait("vector", MSKL, MSKL.n)
                        self.vwait(self.VP.n)
                        gm = self.vop(lambda e, qt=qt, par=par: e.tensor_add(
                            s_sc4[par][:], s_scp[par][:], s_mask[:, qt, :]))
                        last_madd[par] = gm
                        self.vwait(gm)
                        rmx = self.vop(lambda e, par=par: e.reduce_max(
                            s_mx[par][:], s_sc4[par][:], axis=AX.X,
                            negate=True))
                        last_rmx[par] = rmx
                        P.wait("scalar", self.VP, rmx)
                        P.wait("scalar", self.TM, last_tp[par])
                        P.wait("scalar", self.VP, last_rcp[par])
                        ex = self.P.op("scalar", lambda e, par=par:
                                       e.activation(
                                           s_pin[par][:], s_sc4[par][:],
                                           AF.Exp, bias=s_mx[par][:, 0:1],
                                           accum_out=s_rs4[par][:]), IEX)
                        last_exp[par] = ex
                        qstate[qt] = ex

                    def plan_qtB(qt):
                        par = qt % 2
                        ex = qstate.pop(qt)
                        P.wait("tensor", IEX, ex)
                        P.wait("tensor", self.VP, last_evac[par])
                        tpq = None
                        for j in range(8):
                            fn = (lambda e, j=j, par=par: e.transpose(
                                p_t4[par][:, j * 128:(j + 1) * 128],
                                s_pin[par][:, j * 128:(j + 1) * 128],
                                self.s_ident[:]))
                            tpq = self.top(fn) if j == 7 else self.top_(fn)
                        last_tp[par] = tpq
                        P.wait("vector", self.TM, tpq)
                        P.wait("vector", IAV, last_attnv[par])
                        self.vwait(self.VP.n)
                        ptv = self.vop(lambda e, par=par: e.tensor_copy(
                            s_pT4[par][:].rearrange("p a b -> p (a b)"),
                            p_t4[par][:]))
                        last_evac[par] = ptv
                        P.wait("tensor", self.VP, ptv)
                        P.wait("tensor", INRM, last_norm[par])
                        avn = None
                        for kt in range(NST):
                            fn = (lambda e, kt=kt, hp=hp, par=par: e.matmul(
                                (p_o4 if par == 0 else p_g)[:, 0:128],
                                s_pT4[par][:, kt, :], s_vh[hp][:, kt, :],
                                start=(kt == 0), stop=(kt == NST - 1)))
                            avn = (self.P.op("tensor", fn, IAV)
                                   if kt == NST - 1 else self.top_(fn))
                        last_attnv[par] = avn
                        P.wait("vector", IAV, avn)
                        P.wait("vector", self.SP, ex)
                        self.vwait(self.VP.n)
                        v = self.vop(lambda e, par=par: e.reciprocal(
                            s_rc4[par][:], s_rs4[par][:]))
                        last_rcp[par] = v
                        self.vwait(v)
                        nv = self.P.op(
                            "vector",
                            lambda e, qt=qt, h=h, par=par: e.tensor_scalar(
                                s_otok[:, qt, h * 128:(h + 1) * 128],
                                (p_o4 if par == 0 else p_g)[:, 0:128],
                                s_rc4[par][:, 0:1], None, ALU.mult), INRM)
                        last_norm[par] = nv

                    for qt in range(NST):
                        plan_qtA(qt)
                        if qt == 0 and h + 1 < H:
                            plan_iloads(h + 1)
                        if qt >= 1:
                            plan_qtB(qt - 1)
                    plan_qtB(NST - 1)
                    iq_done[h] = IAV.n
                    isc_last[h] = ISC.n
                    plan_asm_tile(2 * h)
                    plan_asm_tile(2 * h + 1)
                inner_attn_nrm = INRM.n
                self.ASM = [ASM]
                self.final_sems.append(ASM)

            self.phase_barrier([MSKL, IQL[0], IQL[1], AIN, ATT, ASM])
            with self.scope():
                s_od = self.sb("s_od", [128, ND, SEL], BF16)
                WOL = [self.sem("WOLa"), self.sem("WOLb")]
                P.wait("tensor", INRM, inner_attn_nrm)
                otp_ev = {}
                for i in range(NST):
                    for half in range(2):
                        idx = 2 * i + half
                        if idx >= 2:
                            P.wait("tensor", self.VP, otp_ev[idx - 2])
                        tpq = None
                        for j in range(8):
                            db = half * 8 + j
                            fn = (lambda e, i=i, half=half, j=j, db=db:
                                  e.transpose(
                                      p_t4[half][:, j * 128:(j + 1) * 128],
                                      s_otok[:, i, db * 128:(db + 1) * 128],
                                      self.s_ident[:]))
                            tpq = self.top(fn) if j == 7 else self.top_(fn)
                        P.wait("vector", self.TM, tpq)
                        otp_ev[idx] = self.vop(
                            lambda e, i=i, half=half: e.tensor_copy(
                                s_od[:, half * 8:(half + 1) * 8,
                                     i * 128:(i + 1) * 128],
                                p_t4[half][:]
                                .rearrange("p (a b) -> p a b", a=8)))
                od_v = self.VP.n
                x2_v = {}
                wo_tm = {}
                for cn in range(4):

                    def guard(par, cn=cn):
                        if cn >= 2:
                            P.wait("sync", self.TM, wo_tm[cn - 2])
                        else:
                            P.wait("sync", self.SP, self.SP.n)
                    wbv, wsem, wn = stream_w_chunk(self.wo_blk, cn, WOL, guard)
                    P.wait("tensor", wsem, wn)
                    P.wait("tensor", self.VP, od_v)
                    for tt in range(NST):
                        ptp = tt % 2
                        key = (cn, tt - 2) if tt >= 2 else (
                            (cn - 1, NST - 2 + tt) if cn > 0 else None)
                        if key is not None:
                            P.wait("tensor", self.VP, x2_v[key])
                        g = None
                        for kc in range(ND):
                            fn = (lambda e, kc=kc, tt=tt, ptp=ptp, wbv=wbv:
                                  e.matmul(
                                      p_q4[ptp][:],
                                      s_od[:, kc, tt * 128:(tt + 1) * 128],
                                      wbv[:, kc, 0:512],
                                      start=(kc == 0), stop=(kc == ND - 1)))
                            g = self.top(fn) if kc == ND - 1 else self.top_(fn)
                        P.wait("vector", self.TM, g)
                        self.vwait(self.VP.n)
                        x2_v[(cn, tt)] = self.vop(
                            lambda e, cn=cn, tt=tt, ptp=ptp: e.tensor_add(
                                s_x2[:, tt, cn * 512:(cn + 1) * 512],
                                s_x2[:, tt, cn * 512:(cn + 1) * 512],
                                p_q4[ptp][:]))
                        if tt == NST - 1:
                            wo_tm[cn] = self.TM.n
                self.x2_done_v = self.VP.n

        self.phase_barrier([])
        x2t = self.tap("x2", [SEL, D], F32)
        self._ytap = self.tap("y", [SEL, D], F32)
        with self.scope():
            s_g2b = s_wb[1][:, 2048:4096]
            s_h2d = self.sb("s_h2d", [128, ND, SEL], BF16)
            s_h2tile = s_wb[1][:, 6144:8192]
            s_u = self.sb("s_u", [128, NF, 512], BF16)
            s_yT = self.sb("s_yT", [128, ND, 512], BF16)
            p_w1 = p_q4[0]
            p_w3 = p_q4[1]
            p_y = [ps("p_y0", [128, 512]), p_g]
            s_silu = self.sb("s_silu4", [128, 512], BF16)
            W14L = [self.sem("W14La"), self.sem("W14Lb")]
            W34L = self.sem("W34L")
            W24L = [self.sem("W24La"), self.sem("W24Lb")]
            YSC = self.sem("YSC")

            if x2t is not None:
                for i in range(NST):
                    P.wait("vector", self.VP, self.x2_done_v)
                    self.vwait(self.VP.n)
                    if i:
                        P.wait("vector", YSC, YSC.n)
                    v = self.vop(lambda e, i=i: e.tensor_copy(
                        s_yfin[:], s_x2[:, i, :]))
                    P.wait("gpsimd", self.VP, v)
                    P.op("gpsimd", lambda e, i=i, x2t=x2t: e.dma_start(
                        out=x2t[128 * i:128 * (i + 1), :], in_=s_yfin[:]),
                        YSC, 16)
                P.wait("vector", YSC, YSC.n)

            g_bcast(self.g2_in, s_g2b)
            h2t = self.tap("h1", [SEL, D], F32)
            H2T = self.sem("H2T")
            P.wait("vector", self.VP, self.x2_done_v)
            h2tp_ev = {}
            last_tpq = 0
            for i in range(NST):
                hv = rms_tile(s_x2[:, i, :], s_g2b, s_h2tile,
                              tm_wait=last_tpq)
                if h2t is not None:
                    P.wait("vector", H2T, H2T.n)
                    self.vwait(hv)
                    cvv = self.vop(lambda e: e.tensor_copy(
                        s_yfin[:], s_h2tile))
                    P.wait("gpsimd", self.VP, cvv)
                    P.op("gpsimd", lambda e, i=i, h2t=h2t: e.dma_start(
                        out=h2t[i * 128:(i + 1) * 128, :], in_=s_yfin[:]),
                        H2T, 16)
                    P.wait("vector", H2T, H2T.n)
                P.wait("tensor", self.VP, hv)
                for half in range(2):
                    idx = 2 * i + half
                    if idx >= 2:
                        P.wait("tensor", self.VP, h2tp_ev[idx - 2])
                    tpq = None
                    for j in range(8):
                        db = half * 8 + j
                        fn = (lambda e, half=half, j=j, db=db: e.transpose(
                            p_t4[half][:, j * 128:(j + 1) * 128],
                            s_h2tile[:, db * 128:(db + 1) * 128],
                            self.s_ident[:]))
                        tpq = self.top(fn) if j == 7 else self.top_(fn)
                    P.wait("vector", self.TM, tpq)
                    h2tp_ev[idx] = self.vop(
                        lambda e, i=i, half=half: e.tensor_copy(
                            s_h2d[:, half * 8:(half + 1) * 8,
                                  i * 128:(i + 1) * 128],
                            p_t4[half][:].rearrange("p (a b) -> p a b", a=8)))
                last_tpq = tpq
            h2d_v = self.VP.n
            h2dt = self.tap("h2d", [128, ND * SEL], BF16)
            if h2dt is not None:
                P.wait("gpsimd", self.VP, h2d_v)
                P.op("gpsimd", lambda e, h2dt=h2dt: e.dma_start(
                    out=h2dt,
                    in_=s_h2d[:].rearrange("p a b -> p (a b)")), H2T, 16)
                P.wait("vector", H2T, H2T.n)

            y_v = {}
            for hh in range(2):
                tok0 = hh * 512
                silu_s = {}
                uv = {}
                g3_tm = {}
                for fb in range(NF):
                    wpar = fb % 2
                    if fb >= 2:
                        P.wait("sync", self.TM, g3_tm[fb - 2])
                    elif hh == 0:
                        P.wait("sync", self.VP, self.VP.n)
                        P.wait("sync", self.SP, self.SP.n)
                        P.wait("sync", self.TM, self.TM.n)
                        P.wait("sync", YSC, YSC.n)
                    else:
                        P.wait("sync", self.TM, self.TM.n)
                        P.wait("sync", YSC, YSC.n)
                    P.op("sync", lambda e, fb=fb, wpar=wpar: e.dma_start(
                        out=s_wb[wpar][:, 0:2048]
                        .rearrange("p (a b) -> p a b", a=16),
                        in_=self.w1_blk[fb].rearrange("k p b -> p k b")),
                        W14L[wpar], 16)
                    P.op("sync", lambda e, fb=fb, wpar=wpar: e.dma_start(
                        out=s_wb[wpar][:, 2048:4096]
                        .rearrange("p (a b) -> p a b", a=16),
                        in_=self.w3_blk[fb].rearrange("k p b -> p k b")),
                        W14L[wpar], 16)
                    w1v = s_wb[wpar][:, 0:2048].rearrange(
                        "p (a b) -> p a b", a=16)
                    w3v = s_wb[wpar][:, 2048:4096].rearrange(
                        "p (a b) -> p a b", a=16)
                    P.wait("tensor", W14L[wpar], W14L[wpar].n)
                    if fb == 0:
                        P.wait("tensor", self.VP, h2d_v)
                    if fb >= 1:
                        P.wait("tensor", self.VP, uv.get(fb - 1, 0))
                    g1m = None
                    g3m = None
                    for kc in range(ND):
                        fn = (lambda e, kc=kc, w1v=w1v, tok0=tok0: e.matmul(
                            p_w1[:], w1v[:, kc, :],
                            s_h2d[:, kc, tok0:tok0 + 512],
                            start=(kc == 0), stop=(kc == ND - 1)))
                        g1m = self.top(fn) if kc == ND - 1 else self.top_(fn)
                    for kc in range(ND):
                        fn = (lambda e, kc=kc, w3v=w3v, tok0=tok0: e.matmul(
                            p_w3[:], w3v[:, kc, :],
                            s_h2d[:, kc, tok0:tok0 + 512],
                            start=(kc == 0), stop=(kc == ND - 1)))
                        g3m = self.top(fn) if kc == ND - 1 else self.top_(fn)
                    g3_tm[fb] = self.TM.n
                    P.wait("scalar", self.TM, g1m)
                    sl = self.P.op("scalar", lambda e: e.activation(
                        s_silu[:], p_w1[:], AF.Silu), self.SP)
                    P.wait("vector", self.TM, g3m)
                    P.wait("vector", self.SP, sl)
                    self.vwait(self.VP.n)
                    uv[fb] = self.vop(lambda e, fb=fb: e.tensor_mul(
                        s_u[:, fb, :], s_silu[:], p_w3[:]))
                u_v = self.VP.n
                ut = self.tap("u", [128, NF * 512], BF16) if hh == 0 else None
                if ut is not None:
                    P.wait("gpsimd", self.VP, u_v)
                    P.op("gpsimd", lambda e, ut=ut: e.dma_start(
                        out=ut,
                        in_=s_u[:].rearrange("p a b -> p (a b)")), YSC, 16)
                    P.wait("tensor", YSC, YSC.n)
                w2_tm = {}
                for dbk in range(ND):
                    wpar = dbk % 2
                    if dbk >= 2:
                        P.wait("sync", self.TM, w2_tm[dbk - 2])
                    else:
                        P.wait("sync", self.TM, self.TM.n)
                        P.wait("sync", YSC, YSC.n)
                    P.op("sync", lambda e, dbk=dbk, wpar=wpar: e.dma_start(
                        out=s_wb[wpar][:].rearrange("p (a b) -> p a b", a=64),
                        in_=self.w2_blk[dbk].rearrange("k p b -> p k b")),
                        W24L[wpar], 16)
                    w2v = s_wb[wpar][:].rearrange("p (a b) -> p a b", a=64)
                    P.wait("tensor", W24L[wpar], W24L[wpar].n)
                    if dbk == 0:
                        P.wait("tensor", self.VP, u_v)
                    pp = dbk % 2
                    if (hh, dbk - 2) in y_v:
                        P.wait("tensor", self.SP, y_v[(hh, dbk - 2)])
                    elif dbk < 2 and hh == 1:
                        P.wait("tensor", self.SP, y_v[(0, ND - 2 + dbk)])
                    g = None
                    for fb in range(NF):
                        fn = (lambda e, fb=fb, pp=pp, w2v=w2v: e.matmul(
                            p_y[pp][:, 0:512], w2v[:, fb, 0:128],
                            s_u[:, fb, :], start=(fb == 0),
                            stop=(fb == NF - 1)))
                        g = self.top(fn) if fb == NF - 1 else self.top_(fn)
                    w2_tm[dbk] = self.TM.n
                    P.wait("scalar", self.TM, g)
                    ysc = self.P.op("scalar",
                                    lambda e, dbk=dbk, pp=pp: e.activation(
                                        s_yT[:, dbk, :], p_y[pp][:],
                                        AF.Copy), self.SP)
                    y_v[(hh, dbk)] = ysc
                P.wait("tensor", self.SP, self.SP.n)
                last_yadd = 0
                for tt in range(4):
                    gt = hh * 4 + tt
                    if last_yadd:
                        P.wait("tensor", self.VP, last_yadd)
                    for half in range(2):
                        tpq = None
                        for j in range(8):
                            db = half * 8 + j
                            fn = (lambda e, tt=tt, db=db, half=half, j=j:
                                  e.transpose(
                                      p_t4[half][:, j * 128:(j + 1) * 128],
                                      s_yT[:, db, tt * 128:(tt + 1) * 128],
                                      self.s_ident[:]))
                            tpq = self.top(fn) if j == 7 else self.top_(fn)
                        P.wait("vector", self.TM, tpq)
                    self.vwait(self.VP.n)
                    P.wait("vector", YSC, YSC.n)
                    v = self.vop(lambda e, gt=gt: e.tensor_add(
                        s_yfin[:, 0:1024], s_x2[:, gt, 0:1024], p_t4[0][:]))
                    self.vwait(v)
                    v = self.vop(lambda e, gt=gt: e.tensor_add(
                        s_yfin[:, 1024:2048],
                        s_x2[:, gt, 1024:2048], p_t4[1][:]))
                    self.vwait(v)
                    last_yadd = self.VP.n
                    v = self.vop(lambda e, gt=gt: e.tensor_scalar(
                        s_yfin[:], s_yfin[:], self.s_rw[:, gt:gt + 1], None,
                        ALU.mult))
                    if getattr(self, "_ytap", None) is not None:
                        P.wait("gpsimd", self.VP, v)
                        P.op("gpsimd", lambda e, gt=gt: e.dma_start(
                            out=self._ytap[gt * 128:(gt + 1) * 128, :],
                            in_=s_yfin[:]), YSC, 16)
                        P.wait("gpsimd", YSC, YSC.n)
                    P.wait("gpsimd", self.VP, v)
                    P.wait("gpsimd", self.ASM[0], self.ASM[0].n)
                    P.op("gpsimd", lambda e, gt=gt: e.indirect_dma_start(
                        out=self.OUT,
                        out_offset=bass.IndirectOffsetOnAxis(
                            ap=self.s_selidx[:, gt:gt + 1], axis=0),
                        in_=s_yfin[:],
                        in_offset=None,
                        compute_op=ALU.add), YSC, 16)
                    P.wait("gpsimd", YSC, YSC.n)
            self.final_sems.append(YSC)
        self.p4_dma_sems = [GSEL, W4L[0], W4L[1], IQS[0], IQS[1], YSC]

    def p5_finish(self):
        pass

    def final_waits(self):
        for s in self.final_sems:
            self.P.wait("sync", s, s.n)


def build_program(debug=(), phases=("p1", "p2", "p3", "p4", "p5")):
    b = Builder(debug=debug, phases=phases)
    nc = b.build()
    return nc, b.dbg_out


_CONSTS = None


def _consts():
    global _CONSTS
    if _CONSTS is None:
        _CONSTS = host_constants()
    return _CONSTS


def make_in_maps(inputs):
    c = _consts()
    shared = dict(
        wqkv=np.ascontiguousarray(inputs["L0_attn_Wqkv_w"]),
        bqkv=np.ascontiguousarray(inputs["L0_attn_Wqkv_b"]).reshape(1, 3 * D),
        maskin=np.ascontiguousarray(inputs["L0_ffn_mask"]),
        freqs=np.ascontiguousarray(inputs["L0_ffn_freqs_cis"]).reshape(SEL, 128),
        wrouter=np.ascontiguousarray(inputs["L0_ffn_w_router"]).reshape(1, D),
        g1=np.ascontiguousarray(inputs["L0_ffn_g1"]).reshape(1, D),
        wq=np.ascontiguousarray(inputs["L0_ffn_wq"]),
        wk=np.ascontiguousarray(inputs["L0_ffn_wk"]),
        wv=np.ascontiguousarray(inputs["L0_ffn_wv"]),
        wo=np.ascontiguousarray(inputs["L0_ffn_wo"]),
        g2=np.ascontiguousarray(inputs["L0_ffn_g2"]).reshape(1, D),
        w1=np.ascontiguousarray(inputs["L0_ffn_w1"]),
        w3=np.ascontiguousarray(inputs["L0_ffn_w3"]),
        w2=np.ascontiguousarray(inputs["L0_ffn_w2"]),
        c_fcos=np.ascontiguousarray(
            np.asarray(inputs["L0_ffn_freqs_cis"])[:, :, 0]),
        c_fsin=np.ascontiguousarray(
            np.asarray(inputs["L0_ffn_freqs_cis"])[:, :, 1]),
        **c,
    )
    x = np.ascontiguousarray(inputs["L0_attn_x"])
    return [dict(x=x[b], **shared) for b in range(B)]


def kernel(**inputs):
    nc, _ = build_program()
    in_maps = make_in_maps(inputs)
    res = run_bass_kernel_spmd(nc, in_maps, core_ids=list(range(B)))
    out = np.stack([np.asarray(r["out"], np.float32) for r in res.results])
    return out
